# version 01
# baseline (speedup 1.0000x reference)
import contextlib
import numpy as np

import concourse.bass as bass
import concourse.mybir as mybir
from concourse.bass_utils import run_bass_kernel_spmd

F32 = mybir.dt.float32
BF16 = mybir.dt.bfloat16
AF = mybir.ActivationFunctionType
ALU = mybir.AluOpType
AX = mybir.AxisListType

DEBUG = False

NCORES = 8
BC = 2
L = 2048
C = 1024
H = 8
DH = 128
TK = BC * L
NT = TK // 128
JC = C // 128
E = 8
FF = 4096
FCN = FF // 128
CAP = 10240
EPS_ATT = 1e-6
EPS_LN = 1e-5

ENGS = ("sync", "scalar", "vector", "tensor", "gpsimd")


class S:
    def __init__(self, h):
        self.h = h
        self.n = 0


class EM:
    def __init__(self):
        self.q = {k: [] for k in ENGS}

    def op(self, eng, fn, inc=None, k=1):
        self.q[eng].append(("op", fn, inc, k))
        if inc is not None:
            inc.n += k

    def wait(self, eng, sem, val=None):
        v = sem.n if val is None else val
        self.q[eng].append(("wait", sem, v, 0))

    def replay(self, block):
        def make(name):
            items = self.q[name]
            def body(e):
                for it in items:
                    if it[0] == "wait":
                        e.wait_ge(it[1].h, it[2])
                    else:
                        ins = it[1](e)
                        if it[2] is not None:
                            ins.then_inc(it[2].h, it[3])
            return body
        for name in ENGS:
            getattr(block, name)(make(name))


def build():
    nc = bass.Bass("TRN2", target_bir_lowering=False)

    okind = "ExternalOutput" if DEBUG else "Internal"
    x_h = nc.dram_tensor("x", [TK, C], F32, kind="ExternalInput").ap()
    wq_h = nc.dram_tensor("wq", [C, C], F32, kind="ExternalInput").ap()
    wk_h = nc.dram_tensor("wk", [C, C], F32, kind="ExternalInput").ap()
    wv_h = nc.dram_tensor("wv", [C, C], F32, kind="ExternalInput").ap()
    bqT_h = nc.dram_tensor("bqT_d", [128, JC], F32, kind="ExternalInput").ap()
    bk_h = nc.dram_tensor("bk", [1, C], F32, kind="ExternalInput").ap()
    bv_h = nc.dram_tensor("bv", [1, C], F32, kind="ExternalInput").ap()
    wrT_h = nc.dram_tensor("wrT_d", [128, JC * E], F32, kind="ExternalInput").ap()
    b1T_h = nc.dram_tensor("b1T_d", [128, E * FCN], F32, kind="ExternalInput").ap()
    b2T_h = nc.dram_tensor("b2T_d", [128, E * JC], F32, kind="ExternalInput").ap()
    w1_h = nc.dram_tensor("w1", [E, C, FF], F32, kind="ExternalInput").ap()
    w2_h = nc.dram_tensor("w2", [E, FF, C], F32, kind="ExternalInput").ap()
    ident_h = nc.dram_tensor("ident_d", [128, 128], F32, kind="ExternalInput").ap()
    triu_h = nc.dram_tensor("triu_d", [128, 128], F32, kind="ExternalInput").ap()
    onesrow_h = nc.dram_tensor("onesrow_d", [1, 128], F32, kind="ExternalInput").ap()
    onesrowb_h = nc.dram_tensor("onesrowb_d", [1, 128], F32, kind="ExternalInput").ap()
    onescol_h = nc.dram_tensor("onescol_d", [128, 1], F32, kind="ExternalInput").ap()
    sel_h = nc.dram_tensor("sel_d", [E, E * 128], F32, kind="ExternalInput").ap()
    cmask_h = nc.dram_tensor("cmask_d", [E, 1], F32, kind="ExternalInput").ap()

    y_h = nc.dram_tensor("y", [TK, C], F32, kind="ExternalOutput").ap()
    if DEBUG:
        xnT_h = nc.dram_tensor("xnT_s", [128, JC, TK], BF16, kind=okind).ap()
        attn_h = nc.dram_tensor("attn_s", [TK, C], F32, kind=okind).ap()
        ffn_h = nc.dram_tensor("ffn_s", [JC, 128, TK], F32, kind=okind).ap()
    else:
        xnT_h = nc.dram_tensor("xnT_s", [128, JC, TK], BF16).ap()
        attn_h = nc.dram_tensor("attn_s", [TK, C], F32).ap()
        ffn_h = nc.dram_tensor("ffn_s", [JC, 128, TK], F32).ap()
    cci_h = nc.dram_tensor("cc_in", [1, E], F32).ap()
    cco_h = nc.dram_tensor("cc_out", [NCORES, E], F32).ap()
    if DEBUG:
        gates_dbg_h = nc.dram_tensor("gates_dbg", [128, NT * E], F32, kind="ExternalOutput").ap()
        pos_dbg_h = nc.dram_tensor("pos_dbg", [128, NT * E], F32, kind="ExternalOutput").ap()
        h_dbg_h = nc.dram_tensor("h_dbg", [128, FCN * 512], BF16, kind="ExternalOutput").ap()
        x4_dbg_h = nc.dram_tensor("x4_dbg", [128, JC * 512], BF16, kind="ExternalOutput").ap()
        w1_dbg_h = nc.dram_tensor("w1_dbg", [128, JC * 1024], BF16, kind="ExternalOutput").ap()
        m_dbg_h = nc.dram_tensor("m_dbg", [128, 512], F32, kind="ExternalOutput").ap()
        pre_dbg_h = nc.dram_tensor("pre_dbg", [128, 512], F32, kind="ExternalOutput").ap()

    es = contextlib.ExitStack()
    with es:
        def sb(name, shape, dt):
            return es.enter_context(nc.sbuf_tensor(name, shape, dt))

        def sem(name):
            return S(es.enter_context(nc.semaphore(name)))

        ident = sb("ident", [128, 128], F32)
        triu = sb("triu", [128, 128], F32)
        onesrow = sb("onesrow", [1, 128], F32)
        onesrowb = sb("onesrowb", [1, 128], BF16)
        onescol = sb("onescol", [128, 1], F32)
        sel = sb("sel", [E, E * 128], BF16)
        cmask = sb("cmask", [E, 1], F32)
        bqT = sb("bqT", [128, JC], F32)
        bkb = sb("bkb", [1, C], BF16)
        bvb = sb("bvb", [1, C], BF16)
        wrT = sb("wrT", [128, JC * E], F32)
        b1T = sb("b1T", [128, E * FCN], F32)
        b2T = sb("b2T", [128, E * JC], F32)

        pos_all = sb("pos_all", [128, NT * E], F32)
        gates_all = sb("gates_all", [128, NT * E], F32)
        kept = sb("kept", [128, NT * E], F32)
        running = sb("running", [1, E], F32)
        countsAll = sb("countsAll", [NCORES, E], F32)
        capm = sb("capm", [1, E], F32)
        capb = sb("capb", [128, E], F32)
        gatesT8 = sb("gatesT8", [E, TK], BF16)

        epsln = sb("epsln", [128, 1], F32)
        wtmp = sb("wtmp", [128, 1], F32)
        xbuf = [sb("xbuf0", [128, C], F32), sb("xbuf1", [128, C], F32)]
        attnbuf = [sb("attnb0", [128, C], F32), sb("attnb1", [128, C], F32)]

        s_sync = sem("s_sync")
        s_sq = sem("s_sq")
        s_gp = sem("s_gp")
        s_cc = sem("s_cc")
        s_v = sem("s_v")
        s_s = sem("s_s")
        s_t = sem("s_t")
        s_out = sem("s_out")

        em = EM()

        def dma(eng, semm, dst, src, accum=False):
            if accum:
                em.op(eng, lambda e, a=dst, b=src: e.dma_start(out=a, in_=b, accum_op=ALU.add),
                      inc=semm, k=16)
            else:
                em.op(eng, lambda e, a=dst, b=src: e.dma_start(out=a, in_=b), inc=semm, k=16)

        dma("sync", s_sync, ident[:, :], ident_h)
        dma("sync", s_sync, triu[:, :], triu_h)
        dma("sync", s_sync, onesrow[:, :], onesrow_h)
        dma("sync", s_sync, onescol[:, :], onescol_h)
        dma("gpsimd", s_gp, sel[:, :], sel_h)
        dma("sync", s_sync, cmask[:, :], cmask_h)
        dma("sync", s_sync, bqT[:, :], bqT_h)
        dma("sync", s_sync, wrT[:, :], wrT_h)
        dma("sync", s_sync, b1T[:, :], b1T_h)
        dma("sync", s_sync, b2T[:, :], b2T_h)
        dma("gpsimd", s_gp, onesrowb[:, :], onesrowb_h)
        dma("gpsimd", s_gp, bkb[:, :], bk_h)
        dma("gpsimd", s_gp, bvb[:, :], bv_h)
        startup_sync = s_sync.n
        startup_gp = s_gp.n

        es_att = contextlib.ExitStack()
        def sba(name, shape, dt):
            return es_att.enter_context(nc.sbuf_tensor(name, shape, dt))
        Wq_sb = sba("Wq_sb", [128, JC * C], BF16)
        Wk_sb = sba("Wk_sb", [128, JC * C], BF16)
        Wv_sb = sba("Wv_sb", [128, JC * C], BF16)
        xnTb = sba("xnTb", [128, JC * L], BF16)
        QT = sba("QT", [128, H * L], BF16)
        Kb = sba("Kb", [128, C], BF16)
        Vs = sba("Vs", [128, H * 132], BF16)
        KVs = sba("KVs", [128, H * 132], BF16)
        tmka = sba("tmka", [128, C], F32)
        tmkb = sba("tmkb", [128, C], F32)
        tq1 = sba("tq1", [128, 512], F32)
        tq2 = sba("tq2", [128, 512], F32)
        zden = sba("zden", [128, H], F32)
        zrec = sba("zrec", [128, H], F32)

        for W_sb, W_h in ((Wq_sb, wq_h), (Wk_sb, wk_h), (Wv_sb, wv_h)):
            src = W_h.rearrange("(j p) d -> p j d", p=128)
            dst = W_sb[:, :].rearrange("p (j d) -> p j d", j=JC)
            dma("gpsimd", s_gp, dst, src)
        wqkv_gp = s_gp.n

        es_p1 = contextlib.ExitStack()
        def sb1(name, shape, dt):
            return es_p1.enter_context(nc.sbuf_tensor(name, shape, dt))
        bstats = sb1("bstats", [128, 12], F32)
        gvar = sb1("gvar", [128, 4], F32)
        gmsq = sb1("gmsq", [128, 4], F32)
        ex2g = sb1("ex2g", [128, 4], F32)
        msum = sb1("msum", [128, 1], F32)
        exs = sb1("exs", [128, 1], F32)
        meanv = sb1("meanv", [128, 1], F32)
        msq2 = sb1("msq2", [128, 1], F32)
        varv = sb1("varv", [128, 1], F32)
        xn = sb1("xn", [128, C], F32)
        xnT32 = sb1("xnT32", [128, C], F32)
        xnTbf = sb1("xnTbf", [128, C], BF16)
        lgs = sb1("lgs", [128, E], F32)
        top8 = sb1("top8", [128, E], F32)
        dpair = sb1("dpair", [128, 2], F32)
        ep_ = sb1("ep", [128, 2], F32)
        ep1 = sb1("ep1", [128, 2], F32)
        gp = sb1("gp", [128, 2], F32)
        eq1 = sb1("eq1", [128, E], F32)
        eq2 = sb1("eq2", [128, E], F32)
        t12 = sb1("t12", [128, E], F32)
        eq2p = sb1("eq2p", [128, E], F32)
        mask01 = sb1("mask01", [128, E], F32)
        m1t = sb1("m1t", [128, E], F32)
        lnv = sb1("lnv", [128, 1], F32)
        rstd = sb1("rstd", [128, 1], F32)
        ptr = [es_p1.enter_context(nc.psum_tensor("ptr0", [128, C], F32)),
               es_p1.enter_context(nc.psum_tensor("ptr1", [128, C], F32))]
        plg = es_p1.enter_context(nc.psum_tensor("plg", [128, E], F32))
        ppos = es_p1.enter_context(nc.psum_tensor("ppos", [128, E], F32))
        ptt = es_p1.enter_context(nc.psum_tensor("ptt", [1, E], F32))

        em.op("vector", lambda e: e.memset(running[:, :], 0.0), inc=s_v)
        em.op("vector", lambda e: e.memset(epsln[:, :], EPS_LN), inc=s_v)
        em.wait("scalar", s_v)
        em.op("scalar", lambda e: e.activation(out=wtmp[:, :], in_=epsln[:, :],
              func=AF.Gelu_apprx_tanh))
        em.op("scalar", lambda e: e.activation(out=wtmp[:, :], in_=epsln[:, :], func=AF.Ln))
        em.op("scalar", lambda e: e.activation(out=wtmp[:, :], in_=epsln[:, :], func=AF.Exp),
              inc=s_s)
        warm_s = s_s.n

        dma("sync", s_sync, xbuf[0][:, :], x_h[0:128, :])
        x_loaded = [s_sync.n, None]
        xc_done = [0, 0]

        ptr_free = [0, 0]
        prev_ppos_copy = 0
        prev_plg_copy = 0
        prev_xnTbf_copy = 0
        prev_tr_done = 0
        prev_ptt_read = 0
        prev_xnT_store = 0

        for t in range(NT):
            bi = t % 2
            if t + 1 < NT:
                em.wait("sync", s_v, xc_done[(t + 1) % 2])
                dma("sync", s_sync, xbuf[(t + 1) % 2][:, :],
                    x_h[(t + 1) * 128:(t + 2) * 128, :])
                x_loaded[(t + 1) % 2] = s_sync.n

            xt = xbuf[bi]
            em.wait("vector", s_sync, x_loaded[bi])
            em.op("vector", lambda e, a=xt: e.bn_stats(out=bstats[:, 0:6], in_=a[:, 0:512]))
            em.op("vector", lambda e, a=xt: e.bn_stats(out=bstats[:, 6:12], in_=a[:, 512:1024]),
                  inc=s_v)
            em.wait("vector", s_v)
            bs3 = bstats[:, :].rearrange("p (g s) -> p g s", s=3)
            em.op("vector", lambda e, bs3=bs3: e.tensor_scalar(
                  out=gvar[:, :].rearrange("p (g s) -> p g s", s=1), in0=bs3[:, :, 2:3],
                  scalar1=1.0 / 256.0, scalar2=None, op0=ALU.mult))
            em.op("vector", lambda e, bs3=bs3: e.tensor_tensor(
                  out=gmsq[:, :].rearrange("p (g s) -> p g s", s=1),
                  in0=bs3[:, :, 1:2], in1=bs3[:, :, 1:2], op=ALU.mult), inc=s_v)
            em.wait("vector", s_v)
            em.op("vector", lambda e: e.tensor_tensor(out=ex2g[:, :], in0=gvar[:, :],
                  in1=gmsq[:, :], op=ALU.add))
            em.op("vector", lambda e, bs3=bs3: e.reduce_sum(
                  out=msum[:, :].rearrange("p (g s) -> p g s", s=1),
                  in_=bs3[:, :, 1:2], axis=AX.XYZW), inc=s_v)
            em.wait("vector", s_v)
            em.op("vector", lambda e: e.reduce_sum(out=exs[:, :], in_=ex2g[:, :], axis=AX.X))
            em.op("vector", lambda e: e.tensor_scalar(out=meanv[:, :], in0=msum[:, :],
                  scalar1=0.25, scalar2=None, op0=ALU.mult), inc=s_v)
            em.wait("vector", s_v)
            em.op("vector", lambda e: e.tensor_tensor(out=msq2[:, :], in0=meanv[:, :],
                  in1=meanv[:, :], op=ALU.mult), inc=s_v)
            em.wait("vector", s_v)
            em.op("vector", lambda e: e.scalar_tensor_tensor(out=varv[:, :], in0=exs[:, :],
                  scalar=0.25, in1=msq2[:, :], op0=ALU.mult, op1=ALU.subtract), inc=s_v)
            em.wait("scalar", s_v)
            em.wait("scalar", s_s, warm_s)
            em.op("scalar", lambda e: e.activation(out=lnv[:, :], in_=varv[:, :],
                  func=AF.Ln, bias=epsln[:, 0:1]), inc=s_s)
            em.op("scalar", lambda e: e.activation(out=rstd[:, :], in_=lnv[:, :],
                  func=AF.Exp, scale=-0.5), inc=s_s)
            em.wait("vector", s_s)
            em.wait("vector", s_t, prev_tr_done)
            em.op("vector", lambda e, a=xt: e.tensor_scalar(out=xn[:, :], in0=a[:, :],
                  scalar1=meanv[:, 0:1], scalar2=rstd[:, 0:1], op0=ALU.subtract,
                  op1=ALU.mult), inc=s_v)
            xc_done[bi] = s_v.n

            em.wait("tensor", s_v)
            em.wait("tensor", s_sync, startup_sync)
            pt = ptr[bi]
            for j in range(JC):
                em.op("tensor", lambda e, j=j, pt=pt: e.transpose(
                    out=pt[:, j * 128:(j + 1) * 128],
                    in_=xn[:, j * 128:(j + 1) * 128], identity=ident[:, :]),
                    inc=(s_t if j == JC - 1 else None))
            tr_done = s_t.n
            prev_tr_done = tr_done
            em.wait("vector", s_t, tr_done)
            em.wait("vector", s_s, prev_xnTbf_copy)
            em.op("vector", lambda e, pt=pt: e.tensor_copy(out=xnT32[:, :], in_=pt[:, :]), inc=s_v)
            ptr_free[bi] = s_v.n
            xnT32_done = s_v.n
            em.wait("scalar", s_v, xnT32_done)
            em.wait("scalar", s_sq, prev_xnT_store)
            em.op("scalar", lambda e: e.copy(out=xnTbf[:, :], in_=xnT32[:, :]), inc=s_s)
            prev_xnTbf_copy = s_s.n
            em.op("scalar", lambda e, t=t: e.dma_start(
                out=xnT_h[:, :, t * 128:(t + 1) * 128],
                in_=xnTbf[:, :].rearrange("p (j t) -> p j t", j=JC)), inc=s_sq, k=16)
            prev_xnT_store = s_sq.n

            em.wait("tensor", s_v, xnT32_done)
            em.wait("tensor", s_v, prev_plg_copy)
            for j in range(JC):
                em.op("tensor", lambda e, j=j: e.matmul(
                    out=plg[:, :], lhsT=xnT32[:, j * 128:(j + 1) * 128],
                    rhs=wrT[:, j * E:(j + 1) * E],
                    start=(j == 0), stop=(j == JC - 1)),
                    inc=(s_t if j == JC - 1 else None))
            plg_mm = s_t.n
            em.wait("vector", s_t, plg_mm)
            em.op("vector", lambda e: e.tensor_copy(out=lgs[:, :], in_=plg[:, :]), inc=s_v)
            prev_plg_copy = s_v.n
            em.wait("vector", s_v)

            em.op("vector", lambda e: e.max(out=top8[:, :], in_=lgs[:, :]), inc=s_v)
            em.wait("vector", s_v)
            em.op("vector", lambda e: e.tensor_tensor(out=dpair[:, 0:1], in0=top8[:, 0:1],
                  in1=top8[:, 1:2], op=ALU.subtract))
            em.op("vector", lambda e: e.tensor_tensor(out=dpair[:, 1:2], in0=top8[:, 1:2],
                  in1=top8[:, 0:1], op=ALU.subtract), inc=s_v)
            em.wait("scalar", s_v)
            em.op("scalar", lambda e: e.activation(out=ep_[:, :], in_=dpair[:, :],
                  func=AF.Exp, scale=-1.0), inc=s_s)
            em.wait("vector", s_s)
            em.op("vector", lambda e: e.tensor_scalar(out=ep1[:, :], in0=ep_[:, :],
                  scalar1=1.0, scalar2=None, op0=ALU.add), inc=s_v)
            em.wait("vector", s_v)
            em.op("vector", lambda e: e.reciprocal(out=gp[:, :], in_=ep1[:, :]), inc=s_v)
            em.wait("vector", s_v)
            em.op("vector", lambda e: e.tensor_scalar(out=eq1[:, :], in0=lgs[:, :],
                  scalar1=top8[:, 0:1], scalar2=None, op0=ALU.is_equal), inc=s_v)
            em.wait("vector", s_v)
            em.op("vector", lambda e: e.tensor_scalar(out=eq2[:, :], in0=lgs[:, :],
                  scalar1=top8[:, 1:2], scalar2=None, op0=ALU.is_equal), inc=s_v)
            em.wait("vector", s_v)
            em.op("vector", lambda e: e.tensor_tensor(out=t12[:, :], in0=eq1[:, :],
                  in1=eq2[:, :], op=ALU.mult), inc=s_v)
            em.wait("vector", s_v)
            em.op("vector", lambda e: e.tensor_tensor(out=eq2p[:, :], in0=eq2[:, :],
                  in1=t12[:, :], op=ALU.subtract), inc=s_v)
            em.wait("vector", s_v)
            em.op("vector", lambda e: e.tensor_tensor(out=mask01[:, :], in0=eq1[:, :],
                  in1=eq2p[:, :], op=ALU.add), inc=s_v)
            mask_done = s_v.n
            em.op("vector", lambda e: e.tensor_scalar(out=m1t[:, :], in0=eq1[:, :],
                  scalar1=gp[:, 0:1], scalar2=None, op0=ALU.mult), inc=s_v)
            em.wait("vector", s_v)
            em.op("vector", lambda e, t=t: e.scalar_tensor_tensor(
                  out=gates_all[:, t * E:(t + 1) * E], in0=eq2p[:, :],
                  scalar=gp[:, 1:2], in1=m1t[:, :], op0=ALU.mult, op1=ALU.add), inc=s_v)

            em.wait("tensor", s_v, mask_done)
            em.wait("tensor", s_v, prev_ppos_copy)
            em.op("tensor", lambda e: e.matmul(out=ppos[:, :], lhsT=triu[:, :],
                  rhs=mask01[:, :], start=True, stop=False))
            em.op("tensor", lambda e: e.matmul(out=ppos[:, :], lhsT=onesrow[:, :],
                  rhs=running[0:1, :], start=False, stop=True), inc=s_t)
            pos_mm = s_t.n
            em.op("tensor", lambda e: e.matmul(out=ptt[:, :], lhsT=onescol[:, :],
                  rhs=mask01[:, :], start=True, stop=True), inc=s_t)
            tt_mm = s_t.n
            em.wait("vector", s_t, pos_mm)
            em.op("vector", lambda e, t=t: e.tensor_copy(
                  out=pos_all[:, t * E:(t + 1) * E], in_=ppos[:, :]), inc=s_v)
            prev_ppos_copy = s_v.n
            em.wait("vector", s_t, tt_mm)
            em.op("vector", lambda e: e.tensor_tensor(out=running[:, :], in0=running[:, :],
                  in1=ptt[0:1, :], op=ALU.add), inc=s_v)
        running_done = s_v.n
        p1_v = s_v.n
        p1_s = s_s.n
        p1_t = s_t.n

        em.wait("gpsimd", s_v, running_done)
        dma("gpsimd", s_gp, cci_h, running[0:1, :])
        em.wait("gpsimd", s_gp)
        em.op("gpsimd", lambda e: e.collective_compute(
            "AllGather", ALU.bypass, replica_groups=[list(range(NCORES))],
            ins=[cci_h], outs=[cco_h]), inc=s_cc)
        em.wait("gpsimd", s_cc)
        dma("gpsimd", s_cc, countsAll[:, :], cco_h, accum=False)
        cc_ready = s_cc.n
        es_p1.close()

        em.wait("tensor", s_gp, wqkv_gp)
        em.wait("vector", s_gp, startup_gp)
        em.wait("vector", s_sync, startup_sync)
        em.op("vector", lambda e: e.memset(
            Vs[:, :].rearrange("p (h z) -> p h z", h=H)[:, :, 128:129], 1.0), inc=s_v)

        attn_store = [0, 0]
        for b in range(BC):
            bt0 = b * L
            em.wait("scalar", s_sq)
            em.wait("scalar", s_t)
            em.op("scalar", lambda e, bt0=bt0: e.dma_start(
                out=xnTb[:, :].rearrange("p (j t) -> p j t", j=JC),
                in_=xnT_h[:, :, bt0:bt0 + L]), inc=s_sq, k=16)
            xnTb_ld = s_sq.n

            es_q = contextlib.ExitStack()
            pq = [es_q.enter_context(nc.psum_tensor(f"pq0_{b}", [128, 512], F32)),
                  es_q.enter_context(nc.psum_tensor(f"pq1_{b}", [128, 512], F32))]
            pq_free = [0, 0]
            em.wait("tensor", s_v)
            em.wait("tensor", s_sq, xnTb_ld)
            for tt in range(L // 512):
                for dq in range(JC):
                    k = tt * JC + dq
                    pqi = pq[k % 2]
                    em.wait("tensor", s_v, pq_free[k % 2])
                    for j in range(JC):
                        em.op("tensor", lambda e, j=j, dq=dq, tt=tt, pqi=pqi: e.matmul(
                            out=pqi[:, :],
                            lhsT=Wq_sb[:, j * C + dq * 128: j * C + (dq + 1) * 128],
                            rhs=xnTb[:, j * L + tt * 512: j * L + (tt + 1) * 512],
                            start=(j == 0), stop=(j == JC - 1)),
                            inc=(s_t if j == JC - 1 else None))
                    qmm = s_t.n
                    em.wait("vector", s_t, qmm)
                    em.op("vector", lambda e, dq=dq, pqi=pqi: e.tensor_scalar(
                        out=tq1[:, :], in0=pqi[:, :], scalar1=bqT[:, dq:dq + 1],
                        scalar2=0.0, op0=ALU.add, op1=ALU.min), inc=s_v)
                    em.wait("scalar", s_v)
                    em.op("scalar", lambda e: e.activation(out=tq2[:, :], in_=tq1[:, :],
                          func=AF.Exp), inc=s_s)
                    em.wait("vector", s_s)
                    em.op("vector", lambda e, dq=dq, pqi=pqi: e.tensor_scalar(
                        out=tq1[:, :], in0=pqi[:, :], scalar1=bqT[:, dq:dq + 1],
                        scalar2=0.0, op0=ALU.add, op1=ALU.max), inc=s_v)
                    em.wait("vector", s_v)
                    em.op("vector", lambda e, dq=dq, tt=tt: e.tensor_tensor(
                        out=QT[:, dq * L + tt * 512: dq * L + (tt + 1) * 512],
                        in0=tq1[:, :], in1=tq2[:, :], op=ALU.add), inc=s_v)
                    pq_free[k % 2] = s_v.n
            qt_done = s_v.n
            es_q.close()

            es_kv = contextlib.ExitStack()
            pk = es_kv.enter_context(nc.psum_tensor(f"pk_{b}", [128, C], F32))
            pv = es_kv.enter_context(nc.psum_tensor(f"pv_{b}", [128, C], F32))
            pkv = es_kv.enter_context(nc.psum_tensor(f"pkv_{b}", [128, H * 256], F32))
            em.wait("tensor", s_v)
            for t in range(L // 128):
                for (W_sb, bb, pdst) in ((Wk_sb, bkb, pk), (Wv_sb, bvb, pv)):
                    for half in range(2):
                        for j in range(JC):
                            em.op("tensor", lambda e, j=j, half=half, t=t, W_sb=W_sb, pdst=pdst: e.matmul(
                                out=pdst[:, half * 512:(half + 1) * 512],
                                lhsT=xnTb[:, j * L + t * 128: j * L + (t + 1) * 128],
                                rhs=W_sb[:, j * C + half * 512: j * C + (half + 1) * 512],
                                start=(j == 0), stop=False))
                        em.op("tensor", lambda e, half=half, bb=bb, pdst=pdst: e.matmul(
                            out=pdst[:, half * 512:(half + 1) * 512],
                            lhsT=onesrowb[:, :],
                            rhs=bb[:, half * 512:(half + 1) * 512],
                            start=False, stop=True),
                            inc=s_t)
                kv_mm = s_t.n
                em.wait("vector", s_t, kv_mm)
                em.op("vector", lambda e: e.tensor_scalar(out=tmka[:, :], in0=pk[:, :],
                      scalar1=0.0, scalar2=None, op0=ALU.min), inc=s_v)
                em.wait("scalar", s_v)
                em.op("scalar", lambda e: e.activation(out=tmkb[:, :], in_=tmka[:, :],
                      func=AF.Exp), inc=s_s)
                em.wait("vector", s_s)
                em.op("vector", lambda e: e.scalar_tensor_tensor(out=Kb[:, :], in0=pk[:, :],
                      scalar=0.0, in1=tmkb[:, :], op0=ALU.max, op1=ALU.add), inc=s_v)
                em.op("vector", lambda e: e.tensor_scalar(
                      out=Vs[:, :].rearrange("p (h z) -> p h z", h=H)[:, :, 0:128],
                      in0=pv[:, :].rearrange("p (h d) -> p h d", h=H),
                      scalar1=1.0 / L, scalar2=None, op0=ALU.mult), inc=s_v)
                kvin_done = s_v.n
                em.wait("tensor", s_v, kvin_done)
                for h in range(H):
                    em.op("tensor", lambda e, h=h, t=t: e.matmul(
                        out=pkv[:, h * 256: h * 256 + 129],
                        lhsT=Kb[:, h * 128:(h + 1) * 128],
                        rhs=Vs[:, h * 132: h * 132 + 129],
                        start=(t == 0), stop=(t == L // 128 - 1)),
                        inc=(s_t if h == H - 1 else None))
            kv_all = s_t.n
            em.wait("vector", s_t, kv_all)
            em.op("vector", lambda e: e.tensor_copy(
                out=KVs[:, :].rearrange("p (h z) -> p h z", h=H)[:, :, 0:129],
                in_=pkv[:, :].rearrange("p (h z) -> p h z", h=H)[:, :, 0:129]), inc=s_v)
            kvs_done = s_v.n
            es_kv.close()

            es_ao = contextlib.ExitStack()
            pao = [es_ao.enter_context(nc.psum_tensor(f"pao0_{b}", [128, H * 256], F32)),
                   es_ao.enter_context(nc.psum_tensor(f"pao1_{b}", [128, H * 256], F32))]
            pao_free = [0, 0]
            em.wait("tensor", s_v, kvs_done)
            for t in range(L // 128):
                bi = t % 2
                pa = pao[bi]
                em.wait("tensor", s_v, pao_free[bi])
                for h in range(H):
                    em.op("tensor", lambda e, h=h, t=t, pa=pa: e.matmul(
                        out=pa[:, h * 256: h * 256 + 129],
                        lhsT=QT[:, h * L + t * 128: h * L + (t + 1) * 128],
                        rhs=KVs[:, h * 132: h * 132 + 129],
                        start=True, stop=True),
                        inc=(s_t if h == H - 1 else None))
                ao_mm = s_t.n
                em.wait("vector", s_t, ao_mm)
                em.op("vector", lambda e, pa=pa: e.tensor_scalar(
                      out=zden[:, :].rearrange("p (h o) -> p h o", o=1),
                      in0=pa[:, :].rearrange("p (h z) -> p h z", h=H)[:, :, 128:129],
                      scalar1=EPS_ATT, scalar2=None, op0=ALU.add), inc=s_v)
                em.wait("vector", s_v)
                em.op("vector", lambda e: e.reciprocal(out=zrec[:, :], in_=zden[:, :]), inc=s_v)
                em.wait("vector", s_v)
                ab = attnbuf[bi]
                em.wait("vector", s_sync, attn_store[bi])
                for h in range(H):
                    em.op("vector", lambda e, h=h, pa=pa, ab=ab: e.tensor_scalar(
                          out=ab[:, h * 128:(h + 1) * 128],
                          in0=pa[:, h * 256: h * 256 + 128],
                          scalar1=zrec[:, h:h + 1], scalar2=float(L),
                          op0=ALU.mult, op1=ALU.mult),
                          inc=(s_v if h == H - 1 else None))
                pao_free[bi] = s_v.n
                em.wait("sync", s_v)
                em.op("sync", lambda e, t=t, bt0=bt0, ab=ab: e.dma_start(
                    out=attn_h[bt0 + t * 128: bt0 + (t + 1) * 128, :], in_=ab[:, :]),
                    inc=s_sync, k=16)
                attn_store[bi] = s_sync.n
            es_ao.close()
        attn_all_stored = s_sync.n
        p3_t = s_t.n
        es_att.close()

        es_p2 = contextlib.ExitStack()
        poff = es_p2.enter_context(nc.psum_tensor("poff", [1, E], F32))
        pcb = es_p2.enter_context(nc.psum_tensor("pcb", [128, E], F32))
        pgT = [es_p2.enter_context(nc.psum_tensor("pgT0", [E, 128], F32)),
               es_p2.enter_context(nc.psum_tensor("pgT1", [E, 128], F32))]
        em.wait("tensor", s_v)
        em.wait("tensor", s_cc, cc_ready)
        em.op("tensor", lambda e: e.matmul(out=poff[:, :], lhsT=cmask[:, :],
              rhs=countsAll[:, :], start=True, stop=True), inc=s_t)
        em.wait("vector", s_t)
        em.op("vector", lambda e: e.tensor_scalar(out=capm[:, :], in0=poff[0:1, :],
              scalar1=-1.0, scalar2=float(CAP), op0=ALU.mult, op1=ALU.add), inc=s_v)
        em.wait("tensor", s_v)
        em.op("tensor", lambda e: e.matmul(out=pcb[:, :], lhsT=onesrow[:, :],
              rhs=capm[0:1, :], start=True, stop=True), inc=s_t)
        em.wait("vector", s_t)
        em.op("vector", lambda e: e.tensor_copy(out=capb[:, :], in_=pcb[:, :]), inc=s_v)
        em.wait("vector", s_v)
        for t in range(NT):
            em.op("vector", lambda e, t=t: e.tensor_tensor(
                  out=kept[:, t * E:(t + 1) * E], in0=pos_all[:, t * E:(t + 1) * E],
                  in1=capb[:, :], op=ALU.is_lt), inc=(s_v if t == NT - 1 else None))
        em.wait("vector", s_v)
        em.op("vector", lambda e: e.tensor_tensor(out=gates_all[:, :], in0=gates_all[:, :],
              in1=kept[:, :], op=ALU.mult), inc=s_v)
        gates_final = s_v.n
        pgT_free = [0, 0]
        for t in range(NT):
            bi = t % 2
            em.wait("tensor", s_v, max(gates_final, pgT_free[bi]))
            em.op("tensor", lambda e, t=t, bi=bi: e.transpose(
                out=pgT[bi][:, :], in_=gates_all[:, t * E:(t + 1) * E],
                identity=ident[:, :]), inc=s_t)
            gmm = s_t.n
            em.wait("vector", s_t, gmm)
            em.op("vector", lambda e, t=t, bi=bi: e.tensor_copy(
                out=gatesT8[:, t * 128:(t + 1) * 128], in_=pgT[bi][:, :]), inc=s_v)
            pgT_free[bi] = s_v.n
        gatesT8_done = s_v.n
        if DEBUG:
            em.wait("sync", s_v, gatesT8_done)
            dma("sync", s_sync, gates_dbg_h, gates_all[:, :])
            dma("sync", s_sync, pos_dbg_h, pos_all[:, :])
        es_p2.close()

        es_p4 = contextlib.ExitStack()
        w2_sb = es_p4.enter_context(nc.sbuf_tensor("w2_sb", [128, FCN * C], BF16))
        w1q = [es_p4.enter_context(nc.sbuf_tensor("w1q0", [128, JC * 1024], BF16)),
               es_p4.enter_context(nc.sbuf_tensor("w1q1", [128, JC * 1024], BF16))]
        xnT4 = [es_p4.enter_context(nc.sbuf_tensor("xnT40", [128, JC * 512], BF16)),
                es_p4.enter_context(nc.sbuf_tensor("xnT41", [128, JC * 512], BF16))]
        h_sb = es_p4.enter_context(nc.sbuf_tensor("h_sb", [128, FCN * 512], BF16))
        m_sb = es_p4.enter_context(nc.sbuf_tensor("m_sb", [128, 512], F32))
        yw = [es_p4.enter_context(nc.sbuf_tensor("yw0", [128, 512], F32)),
              es_p4.enter_context(nc.sbuf_tensor("yw1", [128, 512], F32))]
        dbg_sb = es_p4.enter_context(nc.sbuf_tensor("dbg_sb", [128, 512], F32))
        ph = [es_p4.enter_context(nc.psum_tensor("ph0", [128, 512], F32)),
              es_p4.enter_context(nc.psum_tensor("ph1", [128, 512], F32))]
        pm = es_p4.enter_context(nc.psum_tensor("pm", [128, 512], F32))
        py = es_p4.enter_context(nc.psum_tensor("py", [128, 2048], F32))

        NTT = TK // 512
        w2_ld = 0
        w1q_ld = [0, 0]
        w1q_freemark = [0, 0]
        xnT4_ld = [0, 0]
        xnT4_freemark = [0, 0]
        ph_free = [0, 0]
        yw_free = [0, 0]
        pm_copied = 0
        h_consumed = 0
        py_evac = 0

        def load_w2(e_idx):
            src = w2_h[e_idx].rearrange("(f p) d -> p f d", p=128)
            dst = w2_sb[:, :].rearrange("p (f d) -> p f d", f=FCN)
            dma("gpsimd", s_gp, dst, src)

        def load_w1q(e_idx, qi, buf):
            src = w1_h[e_idx].rearrange("(j p) f -> p j f", p=128)[:, :, qi * 1024:(qi + 1) * 1024]
            dst = w1q[buf][:, :].rearrange("p (j f) -> p j f", j=JC)
            dma("gpsimd", s_gp, dst, src)

        def load_xnT4(tt, buf):
            dma("scalar", s_sq, xnT4[buf][:, :].rearrange("p (j t) -> p j t", j=JC),
                xnT_h[:, :, tt * 512:(tt + 1) * 512])

        em.wait("scalar", s_sync, startup_sync)
        em.op("scalar", lambda e: e.activation(out=wtmp[:, :], in_=epsln[:, :],
              func=AF.Gelu_apprx_tanh), inc=s_s)
        em.wait("gpsimd", s_t, p3_t)
        em.wait("gpsimd", s_v)
        load_w2(0)
        w2_ld = s_gp.n
        load_w1q(0, 0, 0)
        w1q_ld[0] = s_gp.n
        em.wait("scalar", s_t, p3_t)
        em.wait("scalar", s_v)
        load_xnT4(0, 0)
        xnT4_ld[0] = s_sq.n

        for e_idx in range(E):
            for tt in range(NTT):
                it = e_idx * NTT + tt
                xb4 = it % 2
                if it + 1 < E * NTT:
                    nb = (it + 1) % 2
                    em.wait("scalar", s_t, xnT4_freemark[nb])
                    load_xnT4((tt + 1) % NTT, nb)
                    xnT4_ld[nb] = s_sq.n

                em.wait("tensor", s_v, max(gatesT8_done, pm_copied))
                em.wait("tensor", s_sync, startup_sync)
                em.op("tensor", lambda e, e_idx=e_idx, tt=tt: e.matmul(
                    out=pm[:, :], lhsT=sel[:, e_idx * 128:(e_idx + 1) * 128],
                    rhs=gatesT8[:, tt * 512:(tt + 1) * 512], start=True, stop=True), inc=s_t)
                mbc = s_t.n
                em.wait("vector", s_t, mbc)
                em.op("vector", lambda e: e.tensor_copy(out=m_sb[:, :], in_=pm[:, :]), inc=s_v)
                pm_copied = s_v.n

                em.wait("tensor", s_t, h_consumed)
                em.wait("tensor", s_sq, xnT4_ld[xb4])
                if e_idx == 0 and tt == 0:
                    em.wait("tensor", s_gp, w1q_ld[0])
                    for j in range(JC):
                        em.op("tensor", lambda e, j=j: e.matmul(
                            out=ph[0][:, :],
                            lhsT=w1q[0][:, j * 1024: j * 1024 + 128],
                            rhs=xnT4[0][:, j * 512:(j + 1) * 512],
                            start=(j == 0), stop=(j == JC - 1)),
                            inc=(s_t if j == JC - 1 else None))
                for fc in range(FCN):
                    qi = fc // 8
                    fo = fc % 8
                    wb = qi % 2
                    if fo == 0:
                        if fc == 0:
                            em.wait("tensor", s_gp, w1q_ld[wb])
                        nqi = qi + 1
                        if nqi < 4 or tt + 1 < NTT or e_idx + 1 < E:
                            if nqi < 4:
                                ne, ntt_q, nq = e_idx, tt, nqi
                            elif tt + 1 < NTT:
                                ne, ntt_q, nq = e_idx, tt + 1, 0
                            else:
                                ne, ntt_q, nq = e_idx + 1, 0, 0
                            nbw = nqi % 2
                            em.wait("gpsimd", s_t, w1q_freemark[nbw])
                            load_w1q(ne, nq, nbw)
                            w1q_ld[nbw] = s_gp.n
                        if qi > 0:
                            em.wait("tensor", s_gp, w1q_ld[wb])
                    phi = ph[fc % 2]
                    em.wait("tensor", s_s, ph_free[fc % 2])
                    for j in range(JC):
                        em.op("tensor", lambda e, j=j, fo=fo, wb=wb, xb4=xb4, phi=phi: e.matmul(
                            out=phi[:, :],
                            lhsT=w1q[wb][:, j * 1024 + fo * 128: j * 1024 + (fo + 1) * 128],
                            rhs=xnT4[xb4][:, j * 512:(j + 1) * 512],
                            start=(j == 0), stop=(j == JC - 1)),
                            inc=(s_t if j == JC - 1 else None))
                    if fo == 7:
                        w1q_freemark[wb] = s_t.n
                    hmm = s_t.n
                    if DEBUG and e_idx == 0 and tt == 0 and fc == 0:
                        em.wait("vector", s_t, hmm)
                        em.op("vector", lambda e, phi=phi: e.tensor_copy(out=dbg_sb[:, :], in_=phi[:, :]), inc=s_v)
                        predump = s_v.n
                        em.wait("gpsimd", s_v, predump)
                        dma("gpsimd", s_gp, pre_dbg_h, dbg_sb[:, :])
                        dma("gpsimd", s_gp, w1_dbg_h, w1q[0][:, :])
                        dbg_gp = s_gp.n
                        em.wait("scalar", s_gp, dbg_gp)
                        em.wait("tensor", s_gp, dbg_gp)
                    em.wait("scalar", s_t, hmm)
                    em.op("scalar", lambda e, fc=fc, e_idx=e_idx, phi=phi: e.activation(
                        out=h_sb[:, fc * 512:(fc + 1) * 512], in_=phi[:, :],
                        func=AF.Gelu_apprx_tanh, bias=b1T[:, e_idx * FCN + fc: e_idx * FCN + fc + 1],
                        scale=1.0), inc=s_s)
                    ph_free[fc % 2] = s_s.n
                xnT4_freemark[xb4] = s_t.n
                h_done = s_s.n
                if DEBUG and e_idx == 0 and tt == 0:
                    em.wait("sync", s_s, h_done)
                    dma("sync", s_sync, h_dbg_h, h_sb[:, :])
                    dma("sync", s_sync, x4_dbg_h, xnT4[xb4][:, :])
                    dma("sync", s_sync, m_dbg_h, m_sb[:, :])
                    dbg_dump = s_sync.n
                    em.wait("scalar", s_sync, dbg_dump)

                em.wait("tensor", s_s, h_done)
                em.wait("tensor", s_gp, w2_ld)
                for g in range(2):
                    em.wait("tensor", s_v, py_evac)
                    for fc in range(FCN):
                        for dcq in range(4):
                            dc = g * 4 + dcq
                            em.op("tensor", lambda e, fc=fc, dc=dc, dcq=dcq: e.matmul(
                                out=py[:, dcq * 512:(dcq + 1) * 512],
                                lhsT=w2_sb[:, fc * C + dc * 128: fc * C + (dc + 1) * 128],
                                rhs=h_sb[:, fc * 512:(fc + 1) * 512],
                                start=(fc == 0), stop=(fc == FCN - 1)),
                                inc=(s_t if (fc == FCN - 1 and dcq == 3) else None))
                    ymm = s_t.n
                    em.wait("vector", s_t, ymm)
                    em.wait("vector", s_v, pm_copied)
                    for dcq in range(4):
                        dc = g * 4 + dcq
                        bw = dc % 2
                        em.wait("vector", s_gp, yw_free[bw])
                        em.op("vector", lambda e, dc=dc, dcq=dcq, e_idx=e_idx, bw=bw: e.scalar_tensor_tensor(
                            out=yw[bw][:, :], in0=py[:, dcq * 512:(dcq + 1) * 512],
                            scalar=b2T[:, e_idx * JC + dc: e_idx * JC + dc + 1],
                            in1=m_sb[:, :], op0=ALU.add, op1=ALU.mult), inc=s_v)
                        ywv = s_v.n
                        em.wait("gpsimd", s_v, ywv)
                        dst = ffn_h[dc, :, tt * 512:(tt + 1) * 512]
                        if e_idx == 0:
                            dma("gpsimd", s_gp, dst, yw[bw][:, :])
                        else:
                            dma("gpsimd", s_gp, dst, yw[bw][:, :], accum=True)
                        yw_free[bw] = s_gp.n
                    py_evac = s_v.n
                h_consumed = s_t.n
            if e_idx + 1 < E:
                em.wait("gpsimd", s_t)
                load_w2(e_idx + 1)
                w2_ld = s_gp.n
        ffn_all = s_gp.n
        p4_t = s_t.n
        es_p4.close()

        es_p5 = contextlib.ExitStack()
        ffnT = [es_p5.enter_context(nc.sbuf_tensor("ffnT0", [128, C], F32)),
                es_p5.enter_context(nc.sbuf_tensor("ffnT1", [128, C], F32))]
        att_t = [es_p5.enter_context(nc.sbuf_tensor("att0", [128, C], F32)),
                 es_p5.enter_context(nc.sbuf_tensor("att1", [128, C], F32))]
        osb = [es_p5.enter_context(nc.sbuf_tensor("osb0", [128, C], F32)),
               es_p5.enter_context(nc.sbuf_tensor("osb1", [128, C], F32))]
        ptx = [es_p5.enter_context(nc.psum_tensor("ptx0", [128, C], F32)),
               es_p5.enter_context(nc.psum_tensor("ptx1", [128, C], F32))]

        em.wait("sync", s_gp, ffn_all)
        em.wait("sync", s_sync, attn_all_stored)
        em.wait("sync", s_t, p4_t)
        em.wait("sync", s_v)
        em.wait("tensor", s_v)
        ld_done = [0, 0]
        tr_used = [0, 0]
        osb_free = [0, 0]
        ptx_done = [0, 0]
        for t in range(NT):
            bi = t % 2
            em.wait("sync", s_t, tr_used[bi])
            em.wait("sync", s_v, ptx_done[bi])
            for j in range(JC):
                dma("sync", s_sync, ffnT[bi][:, j * 128:(j + 1) * 128],
                    ffn_h[j, :, t * 128:(t + 1) * 128])
            dma("sync", s_sync, att_t[bi][:, :], attn_h[t * 128:(t + 1) * 128, :])
            dma("sync", s_sync, xbuf[bi][:, :], x_h[t * 128:(t + 1) * 128, :])
            ld_done[bi] = s_sync.n
            em.wait("tensor", s_sync, ld_done[bi])
            em.wait("tensor", s_v, ptx_done[bi])
            for j in range(JC):
                em.op("tensor", lambda e, j=j, bi=bi: e.transpose(
                    out=ptx[bi][:, j * 128:(j + 1) * 128],
                    in_=ffnT[bi][:, j * 128:(j + 1) * 128], identity=ident[:, :]),
                    inc=(s_t if j == JC - 1 else None))
            tr_used[bi] = s_t.n
            em.wait("vector", s_t, tr_used[bi])
            em.wait("vector", s_out, osb_free[bi])
            em.op("vector", lambda e, bi=bi: e.tensor_tensor(out=osb[bi][:, :],
                  in0=ptx[bi][:, :], in1=xbuf[bi][:, :], op=ALU.add), inc=s_v)
            em.wait("vector", s_v)
            em.op("vector", lambda e, bi=bi: e.tensor_tensor(out=osb[bi][:, :],
                  in0=osb[bi][:, :], in1=att_t[bi][:, :], op=ALU.add), inc=s_v)
            ptx_done[bi] = s_v.n
            em.wait("sync", s_v, ptx_done[bi])
            em.op("sync", lambda e, t=t, bi=bi: e.dma_start(
                out=y_h[t * 128:(t + 1) * 128, :], in_=osb[bi][:, :]), inc=s_out, k=16)
            osb_free[bi] = s_out.n
        es_p5.close()

        em.wait("gpsimd", s_out, s_out.n)
        em.wait("gpsimd", s_sync, s_sync.n)
        em.wait("gpsimd", s_sq, s_sq.n)
        em.wait("gpsimd", s_gp, s_gp.n)

        block = es.enter_context(nc.Block())
        em.replay(block)
    return nc


def _per_core_inputs(inp):
    x = np.asarray(inp["L0_attn_x"], dtype=np.float32).reshape(16, L, C)
    wq = np.ascontiguousarray(np.asarray(inp["L0_attn_Wq"], dtype=np.float32))
    wk = np.ascontiguousarray(np.asarray(inp["L0_attn_Wk"], dtype=np.float32))
    wv = np.ascontiguousarray(np.asarray(inp["L0_attn_Wv"], dtype=np.float32))
    bq = np.asarray(inp["L0_attn_bq"], dtype=np.float32)
    bk = np.asarray(inp["L0_attn_bk"], dtype=np.float32)
    bv = np.asarray(inp["L0_attn_bv"], dtype=np.float32)
    wr = np.asarray(inp["L0_ffn_wr"], dtype=np.float32)
    w1 = np.ascontiguousarray(np.asarray(inp["L0_ffn_w1"], dtype=np.float32))
    b1 = np.asarray(inp["L0_ffn_b1"], dtype=np.float32)
    w2 = np.ascontiguousarray(np.asarray(inp["L0_ffn_w2"], dtype=np.float32))
    b2 = np.asarray(inp["L0_ffn_b2"], dtype=np.float32)

    bqT = np.ascontiguousarray(bq.reshape(JC, 128).T)
    wrT = np.ascontiguousarray(wr.T.reshape(JC, 128, E).transpose(1, 0, 2).reshape(128, JC * E))
    b1T = np.ascontiguousarray(b1.reshape(E, FCN, 128).transpose(2, 0, 1).reshape(128, E * FCN))
    b2T = np.ascontiguousarray(b2.reshape(E, JC, 128).transpose(2, 0, 1).reshape(128, E * JC))
    ident = np.eye(128, dtype=np.float32)
    triu = np.triu(np.ones((128, 128), dtype=np.float32), 1)
    onesrow = np.ones((1, 128), dtype=np.float32)
    onescol = np.ones((128, 1), dtype=np.float32)
    sel = np.repeat(np.eye(E, dtype=np.float32), 128, axis=1)

    shared = {"wq": wq, "wk": wk, "wv": wv, "bqT_d": bqT, "bk": bk.reshape(1, C),
              "bv": bv.reshape(1, C), "wrT_d": wrT, "b1T_d": b1T, "b2T_d": b2T,
              "w1": w1, "w2": w2, "ident_d": ident, "triu_d": triu,
              "onesrow_d": onesrow, "onesrowb_d": onesrow, "onescol_d": onescol,
              "sel_d": sel}
    maps = []
    for i in range(NCORES):
        m = dict(shared)
        m["x"] = np.ascontiguousarray(x[BC * i: BC * (i + 1)].reshape(TK, C))
        m["cmask_d"] = (np.arange(NCORES) < i).astype(np.float32).reshape(NCORES, 1)
        maps.append(m)
    return maps


def kernel(**inputs):
    nc = build()
    in_maps = _per_core_inputs(inputs)
    res = run_bass_kernel_spmd(nc, in_maps, core_ids=list(range(NCORES)))
    out = np.concatenate([np.asarray(r["y"]) for r in res.results], axis=0)
    return out.reshape(16, L, C)
